# version 07 (final)
# baseline (speedup 1.0000x reference)
import numpy as np
import concourse.bass as bass
import concourse.mybir as mybir
from concourse.bass_utils import run_bass_kernel_spmd
from concourse.alu_op_type import AluOpType as Op
from contextlib import ExitStack

F32 = mybir.dt.float32
F32R = mybir.dt.float32r
BF16 = mybir.dt.bfloat16
I32 = mybir.dt.int32
I16 = mybir.dt.int16
U16 = mybir.dt.uint16
AF = mybir.ActivationFunctionType
AX = mybir.AxisListType

D, T, E, NST, I3 = 1024, 2048, 2048, 16, 3072
NH, NKV, HD = 16, 4, 64
QSCALE = 0.125
NCORES = 8
KD, KE, KI, NT = D // 128, E // 128, I3 // 128, T // 512
EPS = 1e-5
TAPS = {}
TAP_REQ = []
UPTO = ["all"]
PW = [True]
PWSTOP = [""]
DNDBG = [False]
ABTAP = [1, 0]


LD_LANES = [f"l{i}" for i in range(8)]
ST_LANES = [f"m{i}" for i in range(4)]


class P:
    def __init__(self, nc, stack):
        self.nc = nc
        self.sems = {}
        self.ctr = {}
        self.cur = None
        self.cur_name = None
        self.ld_rr = 0
        self.st_rr = 0
        for s in ["st", "sv", "ss", "sg"] + LD_LANES + ST_LANES:
            self.sems[s] = stack.enter_context(nc.semaphore(s))
            self.ctr[s] = 0

    def phase(self, name):
        self.flush()
        self.cur = {k: [] for k in "tvsgy"}
        self.cur_name = name

    def flush(self):
        if self.cur is None:
            return
        engmap = dict(t="tensor", v="vector", s="scalar", g="gpsimd", y="sync")
        q = self.cur
        with self.nc.Block(self.cur_name) as blk:
            for k in "ytvsg":
                if q[k]:
                    getattr(blk, engmap[k])(lambda e, fns=q[k]: [f(e) for f in fns])
        self.cur = None

    def on(self, eng, fn):
        self.cur[eng].append(fn)

    def run(self, eng, build, sem=None, n=1):
        tgt = None
        if sem is not None:
            self.ctr[sem] += n
            tgt = self.ctr[sem]
            h = self.sems[sem]

            def fn(e, build=build, h=h, n=n):
                build(e).then_inc(h, n)
        else:
            def fn(e, build=build):
                build(e)
        self.on(eng, fn)
        return tgt

    def wait(self, eng, sem, v):
        h = self.sems[sem]
        self.on(eng, lambda e, h=h, v=v: e.wait_ge(h, v))

    def wtok(self, eng, tok):
        self.wait(eng, tok[0], tok[1])

    def val(self, sem):
        return self.ctr[sem]

    def dma(self, dst, src, kind="ld"):
        if kind == "ld":
            lane = LD_LANES[self.ld_rr % len(LD_LANES)]
            self.ld_rr += 1
        else:
            lane = ST_LANES[self.st_rr % len(ST_LANES)]
            self.st_rr += 1
        prev = self.ctr[lane]
        if prev:
            self.wait("y", lane, prev)
        cnt = self.inc_ctr(lane, 16)
        h = self.sems[lane]
        self.on("y", lambda y, d=dst, s=src, h=h: y.dma_start(
            out=d, in_=s).then_inc(h, 16))
        return (lane, cnt)

    def inc_ctr(self, sem, n):
        self.ctr[sem] += n
        return self.ctr[sem]

    def runw(self, eng, build):
        sem = {"v": "sv", "s": "ss", "t": "st", "g": "sg"}[eng]
        c = self.run(eng, build, sem)
        self.wait(eng, sem, c)
        return c

    def dma_barrier(self, eng, kind="both"):
        lanes = ([] if kind == "st" else LD_LANES) +                 ([] if kind == "ld" else ST_LANES)
        for lane in lanes:
            if self.ctr[lane]:
                self.wait(eng, lane, self.ctr[lane])


def build_program():
    nc = bass.Bass("TRN2", target_bir_lowering=False,
                   detect_race_conditions=False)
    gstack = ExitStack()
    ge = gstack.enter_context
    p = P(nc, gstack)
    TAPS.clear()

    def din(name, shape, dtype=F32):
        return nc.dram_tensor(name, shape, dtype, kind="ExternalInput").ap()

    def dint(name, shape, dtype=BF16):
        return nc.dram_tensor(name, shape, dtype, kind="Internal").ap()

    x_fm = din("x_fm", [D, T])
    w_norm = din("w_norm", [KD, 128])
    b_norm = din("b_norm", [KD, 128])
    w_in = din("w_in", [D, 2 * E])
    w_conv = din("w_conv", [E, 4])
    b_conv = din("b_conv", [KE, 128])
    A_log = din("A_log", [E, NST])
    w_b = din("w_b", [E, NST])
    w_c = din("w_c", [E, NST])
    w_delta = din("w_delta", [E, E])
    b_delta = din("b_delta", [KE, 128])
    D_param = din("D_param", [KE, 128])
    w_out = din("w_out", [E, D])
    WqT = din("WqT", [D, D])
    WkvT = din("WkvT", [D, 2 * NKV * HD])
    WoT = din("WoT", [D, D])
    wg_n = din("wg_n", [I3, D])
    wu_n = din("wu_n", [I3, D])
    wd_n = din("wd_n", [D, I3])
    wuqT = dint("wuqT", [D, I3], BF16)
    wgqT = dint("wgqT", [D, I3], BF16)
    wdqT = dint("wdqT", [I3, D], F32)
    out_fm = nc.dram_tensor("out_fm", [D, T], F32, kind="ExternalOutput").ap()

    z_sil = dint("z_sil", [E, T])
    f_up = dint("f_up", [I3, T])
    f_gate = dint("f_gate", [I3, T])
    f_u1 = dint("f_u1", [I3, T], I16)
    f_g1 = dint("f_g1", [I3, T], I16)
    f_sv = dint("f_sv", [I3, T], U16)
    f_g2 = dint("f_g2", [I3, T], I16)
    f_o = dint("f_o", [I3, T], BF16)
    cc_bufs = [(nc.dram_tensor(f"cc_in{i}", [1, 4], F32, kind="Internal").ap(),
                nc.dram_tensor(f"cc_out{i}", [1, 4], F32, kind="Internal").ap())
               for i in range(10)]
    d_xp = dint("d_xp", [E, T])
    d_delta = dint("d_delta", [E, T])
    d_dx = dint("d_dx", [E, T])

    def tapme(name, shape, dtype=BF16):
        if name in TAP_REQ:
            TAPS[name] = nc.dram_tensor("tap_" + name, shape, dtype,
                                        kind="ExternalOutput").ap()
            return TAPS[name]
        return None

    onesbf = ge(nc.sbuf_tensor("onesbf", [1, 128], BF16))
    onescol = ge(nc.sbuf_tensor("onescol", [128, 1], BF16))
    onesf = ge(nc.sbuf_tensor("onesf", [128, 128], F32))
    identf = ge(nc.sbuf_tensor("identf", [128, 128], F32))
    identb = ge(nc.sbuf_tensor("identb", [128, 128], BF16))
    murep = ge(nc.sbuf_tensor("murep", [128, T], BF16))
    r12rep = ge(nc.sbuf_tensor("r12rep", [128, T], F32))
    lnf = ge(nc.sbuf_tensor("lnf", [128, T], F32))
    wld = [ge(nc.sbuf_tensor(f"wld{i}", [128, 1536], F32)) for i in range(2)]
    Acols = ge(nc.sbuf_tensor("Acols", [128, KE * NST], F32))
    wcv = ge(nc.sbuf_tensor("wcv", [128, KE * 4], F32))
    bcv = ge(nc.sbuf_tensor("bcv", [128, KE], F32))
    bdel = ge(nc.sbuf_tensor("bdel", [128, KE], F32))
    dpar = ge(nc.sbuf_tensor("dpar", [128, KE], F32))
    gnorm = ge(nc.sbuf_tensor("gnorm", [128, KD], F32))
    bnorm = ge(nc.sbuf_tensor("bnorm", [128, KD], F32))
    wbc_bf = ge(nc.sbuf_tensor("wbc_bf", [128, KE * 2 * NST], BF16))
    ps = [ge(nc.psum_tensor(f"ps{i}", [128, 512], F32)) for i in range(8)]

    def ld(dst, src):
        return p.dma(dst, src, "ld")

    def st_dma(dst, src):
        return p.dma(dst, src, "st")

    def ln_tail(mode, rep_r):
        p.wait("v", "st", p.val("st"))
        for tc in range(NT):
            p.run("v", lambda v, tc=tc: v.tensor_copy(
                lnf[0:1, tc * 512:(tc + 1) * 512], ps[tc][0:1, :]))
            p.run("v", lambda v, tc=tc, rep_r=rep_r: v.tensor_copy(
                rep_r[0:1, tc * 512:(tc + 1) * 512], ps[4 + tc][0:1, :]), "sv")
        p.wait("t", "sv", p.val("sv"))
        for tc in range(NT):
            p.run("t", lambda t, tc=tc: t.matmul(
                ps[tc][:, :], onesf[0:1, 0:128], lnf[0:1, tc * 512:(tc + 1) * 512],
                start=True, stop=True))
            p.run("t", lambda t, tc=tc, rep_r=rep_r: t.matmul(
                ps[4 + tc][:, :], onesf[0:1, 0:128],
                rep_r[0:1, tc * 512:(tc + 1) * 512],
                start=True, stop=True), "st")
        p.wait("v", "st", p.val("st"))
        for tc in range(NT):
            sl = slice(tc * 512, (tc + 1) * 512)
            p.run("v", lambda v, tc=tc, sl=sl, rep_r=rep_r: v.tensor_scalar(
                rep_r[:, sl], ps[tc][:, :], 1.0 / D, None, Op.mult))
            p.run("v", lambda v, tc=tc, sl=sl, rep_r=rep_r: v.tensor_copy(
                murep[:, sl], rep_r[:, sl]))
            p.run("v", lambda v, tc=tc, sl=sl, rep_r=rep_r: v.tensor_tensor(
                lnf[:, sl], rep_r[:, sl], rep_r[:, sl], Op.mult))
            p.run("v", lambda v, tc=tc, sl=sl, rep_r=rep_r: v.scalar_tensor_tensor(
                rep_r[:, sl], ps[4 + tc][:, :], 1.0 / D, lnf[:, sl],
                Op.mult, Op.subtract))
            if mode == "double":
                p.run("v", lambda v, sl=sl, rep_r=rep_r: v.tensor_scalar(
                    lnf[:, sl], rep_r[:, sl], 1.0 + EPS, EPS * EPS, Op.mult,
                    Op.add), "sv")
            else:
                p.run("v", lambda v, sl=sl, rep_r=rep_r: v.tensor_scalar(
                    lnf[:, sl], rep_r[:, sl], 1.0, EPS, Op.mult, Op.add), "sv")
        p.wait("s", "sv", p.val("sv"))
        p.run("s", lambda s, rep_r=rep_r: s.activation(rep_r[:], lnf[:], AF.Ln))
        p.run("s", lambda s, rep_r=rep_r: s.activation(
            rep_r[:], rep_r[:], AF.Exp, bias=0.0, scale=-0.5), "ss")
        p.wait("v", "ss", p.val("ss"))
        p.run("v", lambda v: v.engine_nop(), "sv")
        return p.val("sv")

    def ln_stats_core(get_cast, sq_tile, mode, rep_r):
        for k in range(KD):
            xc, sv_ready = get_cast(k)
            if k >= 1:
                p.wait("s", "st", p.val("st"))
            p.wait("s", "sv", sv_ready)
            p.run("s", lambda s, xc=xc: s.activation(sq_tile[:], xc, AF.Square), "ss")
            p.wait("t", "ss", p.val("ss"))
            for tc in range(NT):
                p.run("t", lambda t, xc=xc, tc=tc, k=k: t.matmul(
                    ps[tc][0:1, :], onescol[:, 0:1],
                    xc[:, tc * 512:(tc + 1) * 512],
                    start=(k == 0), stop=(k == KD - 1)))
                p.run("t", lambda t, tc=tc, k=k: t.matmul(
                    ps[4 + tc][0:1, :], onescol[:, 0:1],
                    sq_tile[:, tc * 512:(tc + 1) * 512],
                    start=(k == 0), stop=(k == KD - 1)),
                    "st" if (tc == NT - 1) else None)
        return ln_tail(mode, rep_r)

    def ln_stats(Xt, xcast, sq_tile, mode, rep_r):
        svs = []
        for k in range(KD):
            svs.append(p.run("v", lambda v, k=k: v.tensor_copy(
                xcast[k][:], Xt[k][:]), "sv"))
        return ln_stats_core(lambda k: (xcast[k][:], svs[k]), sq_tile, mode, rep_r)

    def stream_weight(src, col0, ncols, dst_tiles, row0=0, scale=None):
        p.wait("s", "st", p.val("st"))
        casts = []
        for k in range(len(dst_tiles)):
            i = k % 2
            if k >= 2:
                p.wait("y", "ss", casts[k - 2])
            dyk = ld(wld[i][:, 0:ncols],
                     src[row0 + k * 128:row0 + (k + 1) * 128, col0:col0 + ncols])
            p.wtok("s", dyk)
            if dst_tiles[k].dtype == F32R:
                c = p.run("s", lambda s, k=k, i=i, ncols=ncols: s.activation(
                    dst_tiles[k][:, 0:ncols], wld[i][:, 0:ncols], AF.Copy), "ss")
            elif scale is None:
                c = p.run("s", lambda s, k=k, i=i, ncols=ncols: s.activation(
                    dst_tiles[k][:, 0:ncols], wld[i][:, 0:ncols], AF.Copy), "ss")
            else:
                c = p.run("s", lambda s, k=k, i=i, ncols=ncols: s.activation(
                    dst_tiles[k][:, 0:ncols], wld[i][:, 0:ncols], AF.Copy,
                    bias=0.0, scale=scale), "ss")
            casts.append(c)
        return p.val("ss")

    def stream_weight_bf16(src, col0, ncols, dst_tiles):
        toks = []
        p.wait("y", "st", p.val("st"))
        for k in range(len(dst_tiles)):
            toks.append(ld(dst_tiles[k][:, 0:ncols],
                           src[k * 128:(k + 1) * 128, col0:col0 + ncols]))
        return toks

    def emit_lpbq():
        pw_stack = ExitStack()
        pwe = pw_stack.enter_context
        wrow = pwe(nc.sbuf_tensor("pw_wrow", [128, I3], F32, side="right"))
        pbm = pwe(nc.sbuf_tensor("pw_bm", [128, I3 // 32], F32, side="right"))
        pscl = pwe(nc.sbuf_tensor("pw_s", [128, I3 // 32], F32, side="right"))
        prcp = pwe(nc.sbuf_tensor("pw_r", [128, I3 // 32], F32, side="right"))
        pq32 = pwe(nc.sbuf_tensor("pw_q", [128, I3], I32, side="right"))
        pwq = pwe(nc.sbuf_tensor("pw_wq", [128, I3], F32, side="right"))
        ptst = [pwe(nc.sbuf_tensor(f"pw_t{i}", [128, 128], F32, side="right"))
                for i in range(4)]
        ptstb = [pwe(nc.sbuf_tensor(f"pw_tb{i}", [128, 128], BF16, side="right"))
                 for i in range(4)]
        p.phase("pw")
        p.wait("t", "sv", p.val("sv"))
        p.wait("s", "sv", p.val("sv"))

        def lpbq_one(Wsrc, Tdst, O, I, out_bf):
            nblk = I // 32
            wqs = pwq
            stg = ptstb if out_bf else ptst
            idn = identf
            prev_sts = []
            prev_slot_ss = {}
            slot_tok = {}
            sv_rd = [0]
            for c in range(O // 128):
                if sv_rd[0]:
                    p.wait("y", "sv", sv_rd[0])
                dy = ld(wrow[:, 0:I], Wsrc[c * 128:(c + 1) * 128, :])
                p.wtok("v", dy)
                w3 = wrow[:, 0:I].rearrange("p (b k) -> p b k", k=32)
                p.runw("v", lambda v, w3=w3, nblk=nblk: v.tensor_reduce(
                    pbm[:, 0:nblk].unsqueeze(2), w3, AX.X, Op.max,
                    apply_absolute_value=True))
                p.runw("v", lambda v, nblk=nblk: v.tensor_scalar(
                    pscl[:, 0:nblk], pbm[:, 0:nblk], 1.0 / 7.0, 1e-12,
                    Op.mult, Op.max))
                p.runw("v", lambda v, nblk=nblk: v.reciprocal(
                    prcp[:, 0:nblk], pscl[:, 0:nblk]))
                rb = prcp[:, 0:nblk].unsqueeze(2).broadcast_to([128, nblk, 32])
                sb = pscl[:, 0:nblk].unsqueeze(2).broadcast_to([128, nblk, 32])
                for tok in prev_sts:
                    p.wtok("v", tok)
                prev_sts = []
                sv_t = p.run("v", lambda v, w3=w3, rb=rb: v.tensor_tensor(
                    pwq[:, 0:I].rearrange("p (b k) -> p b k", k=32), w3, rb,
                    Op.mult), "sv")
                sv_rd[0] = sv_t
                p.wait("s", "sv", sv_t)
                ss_q = p.run("s", lambda s, I=I: s.activation(
                    pq32[:, 0:I], pwq[:, 0:I], AF.Copy), "ss")
                p.wait("v", "ss", ss_q)
                sv_w = p.run("v", lambda v, wqs=wqs, sb=sb, I=I, nblk=nblk:
                             v.tensor_tensor(
                    wqs[:, 0:I].rearrange("p (b k) -> p b k", k=32),
                    pq32[:, 0:I].rearrange("p (b k) -> p b k", k=32), sb,
                    Op.mult), "sv")
                if PWSTOP[0] == "quant":
                    continue
                p.wait("t", "sv", sv_w)
                for ti in range(I // 128):
                    slot = ti % 4
                    bank = 6 + (ti % 2)
                    if bank in prev_slot_ss:
                        p.wait("t", "ss", prev_slot_ss[bank])
                    pview = ps[bank][:, 0:128]
                    p.run("t", lambda t, wqs=wqs, ti=ti, pview=pview, idn=idn:
                          t.transpose(pview,
                                      wqs[:, ti * 128:(ti + 1) * 128], idn[:]),
                          "st")
                    st_tr = p.val("st")
                    p.wait("s", "st", st_tr)
                    if slot in slot_tok:
                        p.wtok("s", slot_tok[slot])
                    prev_slot_ss[bank] = p.run(
                        "s", lambda s, stg=stg, slot=slot, pview=pview:
                        s.activation(stg[slot][:], pview, AF.Copy), "ss")
                    if PWSTOP[0] == "notr":
                        continue
                    p.wait("y", "ss", prev_slot_ss[bank])
                    slot_tok[slot] = st_dma(
                        Tdst[ti * 128:(ti + 1) * 128, c * 128:(c + 1) * 128],
                        stg[slot][:])
                    prev_sts.append(slot_tok[slot])
            p.dma_barrier("y", "st")

        lpbq_one(wu_n, wuqT, I3, D, True)
        if PWSTOP[0] not in ("wuonly", "nowd"):
            lpbq_one(wg_n, wgqT, I3, D, True)
        if PWSTOP[0] not in ("wuonly", "nowg"):
            lpbq_one(wd_n, wdqT, D, I3, False)
        p.phase("pw_end")
        pw_stack.close()

    p.phase("p0")
    for k in range(KD):
        ld(gnorm[:, k:k + 1], w_norm[k:k + 1, :])
        ld(bnorm[:, k:k + 1], b_norm[k:k + 1, :])
    for k in range(KE):
        ld(Acols[:, k * NST:(k + 1) * NST], A_log[k * 128:(k + 1) * 128, :])
        ld(wcv[:, k * 4:(k + 1) * 4], w_conv[k * 128:(k + 1) * 128, :])
        ld(bcv[:, k:k + 1], b_conv[k:k + 1, :])
        ld(bdel[:, k:k + 1], b_delta[k:k + 1, :])
        ld(dpar[:, k:k + 1], D_param[k:k + 1, :])
    for k in range(KE):
        ld(lnf[:, k * NST:(k + 1) * NST], w_b[k * 128:(k + 1) * 128, :])
        ld(lnf[:, (KE + k) * NST:(KE + k + 1) * NST], w_c[k * 128:(k + 1) * 128, :])
    DYALL = 1
    p.run("g", lambda g: g.iota(identf[:], [[1, 128]], base=0, channel_multiplier=-1,
                                allow_small_or_imprecise_dtypes=True))
    p.run("g", lambda g: g.memset(onesbf[:], 1.0))
    p.run("g", lambda g: g.memset(onescol[:], 1.0))
    p.run("g", lambda g: g.memset(onesf[:], 1.0), "sg")
    p.wait("v", "sg", p.val("sg"))
    p.run("v", lambda v: v.tensor_scalar(identf[:], identf[:], 0.0, None, Op.is_equal))
    p.run("v", lambda v: v.tensor_copy(identb[:], identf[:]))
    p.dma_barrier("v", "ld")
    p.run("v", lambda v: v.tensor_copy(wbc_bf[:], lnf[:, 0:KE * 2 * NST]), "sv")
    p.dma_barrier("s", "ld")
    p.run("s", lambda s: s.activation(Acols[:], Acols[:], AF.Exp), "ss")
    p.wait("v", "ss", p.val("ss"))
    p.run("v", lambda v: v.tensor_scalar(Acols[:], Acols[:], -1.0, None, Op.mult), "sv")

    if PW[0]:
        emit_lpbq()

    xp_stack = ExitStack()
    xp = [xp_stack.enter_context(nc.sbuf_tensor(f"xp{k}", [128, T], BF16))
          for k in range(KE)]
    p1_stack = ExitStack()
    p1e = p1_stack.enter_context
    xn = [p1e(nc.sbuf_tensor(f"xn{k}", [128, T], BF16, side="right"))
          for k in range(KD)]
    w16 = [p1e(nc.sbuf_tensor(f"w16_{k}", [128, 1024], BF16, side="right"))
           for k in range(KD)]
    xpad = p1e(nc.sbuf_tensor("xpad", [128, T + 8], BF16, side="right"))
    diag4 = p1e(nc.sbuf_tensor("diag4", [128, 4 * 128], BF16, side="right"))
    zstg = [p1e(nc.sbuf_tensor(f"zstg{i}", [128, 512], BF16, side="right"))
            for i in range(2)]
    Xb = [p1e(nc.sbuf_tensor(f"Xb{i}", [128, T], F32, side="right"))
          for i in range(2)]

    p.phase("p1")
    p.dma_barrier("v", "ld")
    xld = []
    xsv = []
    for k in range(KD):
        i = k % 2
        if k >= 2:
            p.wait("y", "sv", xsv[k - 2])
        xld.append(ld(Xb[i][:], x_fm[k * 128:(k + 1) * 128, :]))
        p.wtok("v", xld[k])
        xsv.append(p.run("v", lambda v, k=k, i=i: v.tensor_copy(
            xn[k][:], Xb[i][:]), "sv"))
    sv_reps = ln_stats_core(lambda k: (xn[k][:], xsv[k]), xpad[:, 0:T], "double",
                            r12rep)
    p.wait("v", "sv", sv_reps)
    app_sv = []
    for k in range(KD):
        i = k % 2
        if k >= 2:
            p.wait("y", "sv", app_sv[k - 2])
        else:
            p.wait("y", "sv", xsv[KD - 2 + k])
        dxk = ld(Xb[i][:], x_fm[k * 128:(k + 1) * 128, :])
        p.wtok("v", dxk)
        p.run("v", lambda v, k=k, i=i: v.tensor_tensor(
            lnf[:], Xb[i][:], murep[:], Op.subtract))
        p.run("v", lambda v, k=k: v.scalar_tensor_tensor(
            lnf[:], lnf[:], gnorm[:, k:k + 1], r12rep[:], Op.mult, Op.mult))
        app_sv.append(p.run("v", lambda v, k=k: v.tensor_scalar(
            xn[k][:], lnf[:], bnorm[:, k:k + 1], None, Op.add), "sv"))
    sv_xn = p.val("sv")
    tap_x1ln = tapme("x1ln", [D, T])
    if tap_x1ln is not None:
        p.wait("y", "sv", sv_xn)
        for k in range(KD):
            st_dma(tap_x1ln[k * 128:(k + 1) * 128, :], xn[k][:])
    tap_r12 = tapme("r12", [128, T], F32)
    if tap_r12 is not None:
        p.wait("y", "sv", sv_reps)
        st_dma(tap_r12[:, :], r12rep[:])
    tap_mu = tapme("mu", [128, T])
    if tap_mu is not None:
        p.wait("y", "sv", sv_reps)
        st_dma(tap_mu[:, :], murep[:])
    p.run("v", lambda v: v.memset(xpad[:, 0:8], 0.0), "sv")

    prev_ss_evac = 0
    prev_st_conv = 0
    for sweep in range(2):
        p.phase(f"p1_x{sweep}")
        cast_done = stream_weight(w_in, sweep * 1024, 1024, w16)
        p.wait("t", "ss", cast_done)
        for mm_ in range(8):
            m = sweep * 8 + mm_
            if prev_ss_evac:
                p.wait("t", "ss", prev_ss_evac)
            for tc in range(NT):
                for k in range(KD):
                    p.run("t", lambda t, mm_=mm_, tc=tc, k=k: t.matmul(
                        ps[tc][:, :], w16[k][:, mm_ * 128:(mm_ + 1) * 128],
                        xn[k][:, tc * 512:(tc + 1) * 512],
                        start=(k == 0), stop=(k == KD - 1)),
                        "st" if k == KD - 1 else None)
            st_mm = p.val("st")
            if prev_st_conv:
                p.wait("v", "st", prev_st_conv)
            for j in range(4):
                p.run("v", lambda v, m=m, j=j: v.tensor_scalar(
                    diag4[:, j * 128:(j + 1) * 128], identf[:],
                    wcv[:, m * 4 + j:m * 4 + j + 1], None, Op.mult), "sv")
            sv_diag = p.val("sv")
            if prev_st_conv:
                p.wait("s", "st", prev_st_conv)
            p.wait("s", "st", st_mm)
            for tc in range(NT):
                p.run("s", lambda s, tc=tc: s.activation(
                    xpad[:, 8 + tc * 512:8 + (tc + 1) * 512], ps[tc][:, :], AF.Copy),
                    "ss")
            prev_ss_evac = p.val("ss")
            p.wait("t", "ss", prev_ss_evac)
            p.wait("t", "sv", sv_diag)
            for tc in range(NT):
                for j in range(4):
                    p.run("t", lambda t, tc=tc, j=j: t.matmul(
                        ps[4 + (tc % 2) * 2][:, :], diag4[:, j * 128:(j + 1) * 128],
                        xpad[:, 5 + tc * 512 + j:5 + tc * 512 + j + 512],
                        start=(j == 0), stop=(j == 3)), "st" if j == 3 else None)
                st_conv = p.val("st")
                p.wait("s", "st", st_conv)
                p.run("s", lambda s, m=m, tc=tc: s.activation(
                    xp[m][:, tc * 512:(tc + 1) * 512], ps[4 + (tc % 2) * 2][:, :],
                    AF.Silu, bias=bcv[:, m:m + 1], scale=1.0), "ss")
            prev_st_conv = p.val("st")

    zrounds = []
    zz_ss = {}
    for sweep in range(2):
        p.phase(f"p1_z{sweep}")
        cast_done = stream_weight(w_in, E + sweep * 1024, 1024, w16)
        p.wait("t", "ss", cast_done)
        for mm_ in range(8):
            m = sweep * 8 + mm_
            for tc in range(NT):
                if tc in zz_ss:
                    p.wait("t", "ss", zz_ss[tc])
                for k in range(KD):
                    p.run("t", lambda t, mm_=mm_, tc=tc, k=k: t.matmul(
                        ps[tc][:, :], w16[k][:, mm_ * 128:(mm_ + 1) * 128],
                        xn[k][:, tc * 512:(tc + 1) * 512],
                        start=(k == 0), stop=(k == KD - 1)),
                        "st" if k == KD - 1 else None)
                st_mm = p.val("st")
                r = len(zrounds)
                i = r % 2
                if r >= 2:
                    p.wtok("s", zrounds[r - 2])
                p.wait("s", "st", st_mm)
                zz_ss[tc] = p.run("s", lambda s, i=i, tc=tc: s.activation(
                    zstg[i][:], ps[tc][:, :], AF.Silu), "ss")
                p.wait("y", "ss", zz_ss[tc])
                zrounds.append(st_dma(
                    z_sil[m * 128:(m + 1) * 128, tc * 512:(tc + 1) * 512], zstg[i][:]))

    tap_zsil = tapme("zsil", [E, T])
    if tap_zsil is not None:
        p.dma_barrier("y", "st")
        st_dma(tap_zsil[:, :], z_sil[:, :])
    tap_xp = tapme("xp", [E, T])
    if tap_xp is not None:
        p.wait("y", "ss", p.val("ss"))
        for k in range(KE):
            st_dma(tap_xp[k * 128:(k + 1) * 128, :], xp[k][:])

    p.phase("p2_bc")
    p1_stack.close()
    bc_stack = ExitStack()
    bcseq = bc_stack.enter_context(
        nc.sbuf_tensor("bcseq", [64, T], BF16, side="right"))
    sv_bcz = p.run("v", lambda v: v.memset(bcseq[0:64, :], 0.0), "sv")
    sv_bc_prev = 0
    for tc in range(NT):
        if sv_bc_prev:
            p.wait("t", "sv", sv_bc_prev)
        for k in range(KE):
            last = (k == KE - 1)
            p.run("t", lambda t, k=k, tc=tc, last=last: t.matmul(
                ps[0][0:NST, :], wbc_bf[:, k * NST:(k + 1) * NST],
                xp[k][:, tc * 512:(tc + 1) * 512], start=(k == 0), stop=last))
            p.run("t", lambda t, k=k, tc=tc, last=last: t.matmul(
                ps[1][32:32 + NST, :],
                wbc_bf[:, KE * NST + k * NST:KE * NST + (k + 1) * NST],
                xp[k][:, tc * 512:(tc + 1) * 512], start=(k == 0), stop=last),
                "st" if last else None)
        st_bc = p.val("st")
        p.wait("v", "st", st_bc)
        p.run("v", lambda v, tc=tc: v.tensor_copy(
            bcseq[0:NST, tc * 512:(tc + 1) * 512], ps[0][0:NST, :]))
        sv_bc_prev = p.run("v", lambda v, tc=tc: v.tensor_copy(
            bcseq[32:32 + NST, tc * 512:(tc + 1) * 512], ps[1][32:32 + NST, :]),
            "sv")
    tap_bc = tapme("bcseq", [2 * NST, T])
    if tap_bc is not None:
        p.wait("y", "sv", p.val("sv"))
        st_dma(tap_bc[0:NST, :], bcseq[0:NST, :])
        st_dma(tap_bc[NST:2 * NST, :], bcseq[32:32 + NST, :])

    d_stack = ExitStack()
    de = d_stack.enter_context
    w16d = [de(nc.sbuf_tensor(f"w16d{k}", [128, 1024], BF16, side="right"))
            for k in range(KE)]
    dstg = [de(nc.sbuf_tensor(f"dstg{i}", [128, T], BF16, side="right"))
            for i in range(2)]
    xstg = [de(nc.sbuf_tensor(f"xstg{i}", [128, T], BF16, side="right"))
            for i in range(2)]
    tap_delta = tapme("delta", [E, T])
    drounds = []
    for sweep in range(2):
        p.phase(f"p2_delta{sweep}")
        cast_wd = stream_weight(w_delta, sweep * 1024, 1024, w16d)
        p.wait("t", "ss", cast_wd)
        dsv = {}
        for mm_ in range(8):
            m = sweep * 8 + mm_
            r = len(drounds)
            i = r % 2
            for tc in range(NT):
                if tc in dsv:
                    p.wait("t", "sv", dsv[tc])
                for k in range(KE):
                    p.run("t", lambda t, mm_=mm_, tc=tc, k=k: t.matmul(
                        ps[tc][:, :], w16d[k][:, mm_ * 128:(mm_ + 1) * 128],
                        xp[k][:, tc * 512:(tc + 1) * 512],
                        start=(k == 0), stop=(k == KE - 1)),
                        "st" if k == KE - 1 else None)
                st_d = p.val("st")
                p.wait("v", "st", st_d)
                if r >= 2:
                    p.wtok("v", drounds[r - 2][0])
                    p.wtok("v", drounds[r - 2][1])
                p.run("v", lambda v, m=m, tc=tc: v.tensor_scalar(
                    lnf[:, 0:512], ps[tc][:, :], bdel[:, m:m + 1], 80.0, Op.add,
                    Op.min), "sv")
                p.wait("s", "sv", p.val("sv"))
                p.run("s", lambda s: s.activation(lnf[:, 512:1024], lnf[:, 0:512],
                                                  AF.Exp))
                p.run("s", lambda s: s.activation(
                    lnf[:, 0:512], lnf[:, 512:1024], AF.Ln, bias=1.0, scale=1.0), "ss")
                p.wait("v", "ss", p.val("ss"))
                dsv[tc] = p.run("v", lambda v, i=i, m=m, tc=tc: v.scalar_tensor_tensor(
                    dstg[i][:, tc * 512:(tc + 1) * 512], ps[tc][:, :],
                    bdel[:, m:m + 1], lnf[:, 0:512], Op.add, Op.max), "sv")
            sv_dx = p.run("v", lambda v, i=i, m=m: v.tensor_tensor(
                xstg[i][:], dstg[i][:], xp[m][:], Op.mult), "sv")
            p.wait("y", "sv", sv_dx)
            t1 = st_dma(d_delta[m * 128:(m + 1) * 128, :], dstg[i][:])
            t2 = st_dma(d_dx[m * 128:(m + 1) * 128, :], xstg[i][:])
            if tap_delta is not None:
                t1 = st_dma(tap_delta[m * 128:(m + 1) * 128, :], dstg[i][:])
            drounds.append((t1, t2))

    p.phase("p2_spill")
    for k in range(KE):
        st_dma(d_xp[k * 128:(k + 1) * 128, :], xp[k][:])
    p.dma_barrier("y", "st")
    p.phase("p2_spill_end")
    d_stack.close()
    xp_stack.close()

    y_stack = ExitStack()
    ybf = [y_stack.enter_context(nc.sbuf_tensor(f"ybf{k}", [128, T], BF16))
           for k in range(KE)]
    s_stack = ExitStack()
    se = s_stack.enter_context
    brep4 = [se(nc.sbuf_tensor(f"brep{j}", [128, T], BF16, side="right"))
             for j in range(4)]
    crep4 = [se(nc.sbuf_tensor(f"crep{j}", [128, T], BF16, side="right"))
             for j in range(4)]
    dstr = [se(nc.sbuf_tensor(f"dstr{i}", [128, T], BF16, side="right"))
            for i in range(2)]
    xstr = [se(nc.sbuf_tensor(f"xstr{i}", [128, T], BF16, side="right"))
            for i in range(2)]
    dAb = [se(nc.sbuf_tensor(f"dAb{i}", [128, T], F32, side="right"))
           for i in range(2)]
    ub = [se(nc.sbuf_tensor(f"ub{i}", [128, T], BF16, side="right"))
          for i in range(2)]
    hbuf = [se(nc.sbuf_tensor(f"hbuf{i}", [128, T], BF16, side="right"))
            for i in range(2)]
    hcb = se(nc.sbuf_tensor("hcb", [128, T], BF16, side="right"))
    ind32 = se(nc.sbuf_tensor("ind32", [64, 32 * 128], BF16, side="right"))

    tapme("reps", [8 * 128, T])
    p.phase("p3_scan")
    for h in range(2):
        p.wait("g", "sv", p.val("sv"))
        p.run("g", lambda g, h=h: g.iota(
            lnf[0:64, 0:16 * 128], [[1, 16], [0, 128]], base=32 * h,
            channel_multiplier=-1, allow_small_or_imprecise_dtypes=True), "sg")
        p.wait("v", "sg", p.val("sg"))
        p.run("v", lambda v, h=h: v.tensor_scalar(
            ind32[:, h * 16 * 128:(h + 1) * 16 * 128], lnf[0:64, 0:16 * 128],
            0.0, None, Op.is_equal), "sv")
    p.wait("t", "sv", p.val("sv"))
    p.wait("t", "sv", sv_bcz)
    rep_ss_prev = 0
    stream_used = {}
    flat_done = {}
    flat = 0
    for g in range(4):
        if g >= 1:
            p.wait("s", "sv", flat_done[flat - 1])
            p.wait("t", "sv", flat_done[flat - 1])
        for j in range(4):
            n = 4 * g + j
            for tc in range(NT):
                if rep_ss_prev:
                    p.wait("t", "ss", rep_ss_prev)
                p.run("t", lambda t, n=n, tc=tc: t.matmul(
                    ps[6][:, :], ind32[:, n * 128:(n + 1) * 128],
                    bcseq[0:64, tc * 512:(tc + 1) * 512], start=True, stop=True))
                p.run("t", lambda t, n=n, tc=tc: t.matmul(
                    ps[7][:, :], ind32[:, (NST + n) * 128:(NST + n + 1) * 128],
                    bcseq[0:64, tc * 512:(tc + 1) * 512],
                    start=True, stop=True), "st")
                st_rep = p.val("st")
                p.wait("s", "st", st_rep)
                p.run("s", lambda s, j=j, tc=tc: s.activation(
                    brep4[j][:, tc * 512:(tc + 1) * 512], ps[6][:, :], AF.Copy))
                rep_ss_prev = p.run("s", lambda s, j=j, tc=tc: s.activation(
                    crep4[j][:, tc * 512:(tc + 1) * 512], ps[7][:, :], AF.Copy), "ss")
        ss_rep = p.val("ss")
        if g == 0 and "reps" in TAPS:
            p.wait("y", "ss", ss_rep)
            for j in range(4):
                st_dma(TAPS["reps"][j * 128:(j + 1) * 128, :], brep4[j][:])
                st_dma(TAPS["reps"][(4 + j) * 128:(5 + j) * 128, :], crep4[j][:])
        p.wait("v", "ss", ss_rep)
        for k in range(KE):
            r = g * KE + k
            i = r % 2
            if r >= 2:
                ss_c, sv_c = stream_used[r - 2]
                p.wait("y", "ss", ss_c)
                p.wait("y", "sv", sv_c)
            dz1 = ld(dstr[i][:], d_delta[k * 128:(k + 1) * 128, :])
            dz2 = ld(xstr[i][:], d_dx[k * 128:(k + 1) * 128, :])
            last_ss = last_sv = None
            for j in range(4):
                n = 4 * g + j
                fb = flat % 2
                if flat >= 2:
                    p.wait("s", "sv", flat_done[flat - 2])
                p.wtok("s", dz1)
                last_ss = p.run("s", lambda s, k=k, n=n, i=i, fb=fb: s.activation(
                    dAb[fb][:], dstr[i][:], AF.Exp, bias=0.0,
                    scale=Acols[:, k * NST + n:k * NST + n + 1]), "ss")
                p.wtok("v", dz2)
                p.run("v", lambda v, i=i, j=j, fb=fb: v.tensor_tensor(
                    ub[fb][:], xstr[i][:], brep4[j][:], Op.mult))
                p.wait("v", "ss", last_ss)
                p.run("v", lambda v, fb=fb: v.tensor_tensor_scan(
                    hbuf[fb][:], dAb[fb][:], ub[fb][:], 0.0, Op.mult, Op.add))
                p.run("v", lambda v, fb=fb, j=j: v.tensor_tensor(
                    hcb[:], hbuf[fb][:], crep4[j][:], Op.mult))
                if g == 0 and j == 0:
                    sd = p.run("v", lambda v, k=k: v.tensor_copy(ybf[k][:], hcb[:]),
                               "sv")
                else:
                    sd = p.run("v", lambda v, k=k: v.tensor_tensor(
                        ybf[k][:], ybf[k][:], hcb[:], Op.add), "sv")
                flat_done[flat] = sd
                last_sv = sd
                flat += 1
            stream_used[r] = (last_ss, last_sv)

    tap_ybf = tapme("ybf", [E, T])
    if tap_ybf is not None:
        p.phase("p3_ybftap")
        p.wait("y", "sv", p.val("sv"))
        for k in range(KE):
            st_dma(tap_ybf[k * 128:(k + 1) * 128, :], ybf[k][:])
    p.phase("p3_fin")
    fin_sv = {}
    for k in range(KE):
        i = k % 2
        if k >= 2:
            p.wait("y", "sv", fin_sv[k - 2])
        dxp = ld(xstr[i][:], d_xp[k * 128:(k + 1) * 128, :])
        dzz = ld(dstr[i][:], z_sil[k * 128:(k + 1) * 128, :])
        p.wtok("v", dxp)
        p.run("v", lambda v, k=k, i=i: v.scalar_tensor_tensor(
            lnf[:], xstr[i][:], dpar[:, k:k + 1], ybf[k][:], Op.mult, Op.add))
        p.wtok("v", dzz)
        fin_sv[k] = p.run("v", lambda v, k=k, i=i: v.tensor_tensor(
            ybf[k][:], lnf[:], dstr[i][:], Op.mult), "sv")
    sv_ya = p.val("sv")
    tap_ya = tapme("ya", [E, T])
    if tap_ya is not None:
        p.wait("y", "sv", sv_ya)
        for k in range(KE):
            st_dma(tap_ya[k * 128:(k + 1) * 128, :], ybf[k][:])
    p.phase("p3_fin_end")
    s_stack.close()
    bc_stack.close()

    o_stack = ExitStack()
    outa = [o_stack.enter_context(
        nc.sbuf_tensor(f"outa{m}", [128, T], BF16, side="right"))
        for m in range(KD)]
    w_stack = ExitStack()
    w16o = [w_stack.enter_context(
        nc.sbuf_tensor(f"w16o{k}", [128, 1024], BF16, side="right"))
        for k in range(KE)]
    p.phase("p3_wout")
    cast_wo = stream_weight(w_out, 0, 1024, w16o)
    p.wait("t", "ss", cast_wo)
    p.wait("t", "sv", sv_ya)
    wsv = 0
    for m in range(KD):
        if wsv:
            p.wait("t", "sv", wsv)
        for tc in range(NT):
            for k in range(KE):
                p.run("t", lambda t, m=m, tc=tc, k=k: t.matmul(
                    ps[tc][:, :], w16o[k][:, m * 128:(m + 1) * 128],
                    ybf[k][:, tc * 512:(tc + 1) * 512],
                    start=(k == 0), stop=(k == KE - 1)),
                    "st" if k == KE - 1 else None)
        st_o = p.val("st")
        p.wait("v", "st", st_o)
        for tc in range(NT):
            p.run("v", lambda v, m=m, tc=tc: v.tensor_copy(
                outa[m][:, tc * 512:(tc + 1) * 512], ps[tc][:, :]), "sv")
        wsv = p.val("sv")
    sv_outa = p.val("sv")
    p.phase("p3_wout_end")
    w_stack.close()
    y_stack.close()

    X2 = [ge(nc.sbuf_tensor(f"X2_{k}", [128, T], F32)) for k in range(KD)]
    xc2_stack = ExitStack()
    xc2 = [xc2_stack.enter_context(
        nc.sbuf_tensor(f"xc2_{k}", [128, T], BF16, side="right"))
        for k in range(KD)]
    sq2 = xc2_stack.enter_context(
        nc.sbuf_tensor("sq2", [128, T], BF16, side="right"))
    p.phase("p3_resid")
    for k in range(KD):
        ld(X2[k][:], x_fm[k * 128:(k + 1) * 128, :])
    p.dma_barrier("v", "ld")
    sv_reps2 = ln_stats(X2, xc2, sq2, "plain", r12rep)
    p.wait("v", "sv", max(sv_reps2, sv_outa))
    for k in range(KD):
        p.run("v", lambda v, k=k: v.tensor_tensor(lnf[:], X2[k][:], murep[:],
                                                  Op.subtract))
        p.run("v", lambda v, k=k: v.tensor_tensor(lnf[:], lnf[:], r12rep[:], Op.mult))
        p.run("v", lambda v, k=k: v.tensor_tensor(X2[k][:], X2[k][:], lnf[:], Op.add))
        p.run("v", lambda v, k=k: v.tensor_tensor(X2[k][:], X2[k][:], outa[k][:],
                                                  Op.add), "sv")
    sv_x1 = p.val("sv")
    p.phase("p3_end")
    xc2_stack.close()
    o_stack.close()


    cc_idx = [0]

    def all_reduce_max(nvals, srow):
        ci, co = cc_bufs[cc_idx[0]]
        cc_idx[0] += 1
        p.wait("y", "sv", p.val("sv"))
        dd_in = st_dma(ci[0:1, 0:nvals], srow[0:1, 0:nvals])
        p.wtok("g", dd_in)
        gc = p.run("g", lambda g, ci=ci, co=co, nvals=nvals: g.collective_compute(
            "AllReduce", Op.max, replica_groups=[[0, 1, 2, 3, 4, 5, 6, 7]],
            ins=[ci[0:1, 0:nvals]], outs=[co[0:1, 0:nvals]]), "sg")
        p.wait("y", "sg", gc)
        return ld(srow[0:1, 0:nvals], co[0:1, 0:nvals])

    def repl_col(srow, j, dst_col):
        p.wait("t", "sv", p.val("sv"))
        p.run("t", lambda t, srow=srow, j=j: t.matmul(
            ps[7][:, 0:1], onesf[0:1, 0:128], srow[0:1, j:j + 1],
            start=True, stop=True), "st")
        p.wait("v", "st", p.val("st"))
        return p.run("v", lambda v, dst_col=dst_col: v.tensor_copy(
            dst_col, ps[7][:, 0:1]), "sv")

    def emit_ffn(tag):
        stop = UPTO[0]
        f_stack = ExitStack()
        fe = f_stack.enter_context
        wf = [fe(nc.sbuf_tensor(f"{tag}wf{k}", [128, 1536], BF16, side="right"))
              for k in range(KD)]
        fst = [fe(nc.sbuf_tensor(f"{tag}fst{i}", [128, 512], BF16, side="right"))
               for i in range(2)]
        cols = fe(nc.sbuf_tensor(f"{tag}cols", [128, 16], F32, side="right"))
        srow = fe(nc.sbuf_tensor(f"{tag}srow", [1, 16], F32, side="right"))
        mcol = fe(nc.sbuf_tensor(f"{tag}mcol", [128, 4], F32, side="right"))
        mid_stack = ExitStack()
        me = mid_stack.enter_context
        fin1 = [me(nc.sbuf_tensor(f"{tag}fin{i}", [128, T], BF16, side="right"))
                for i in range(3)]
        iu16 = me(nc.sbuf_tensor(f"{tag}iu16", [128, T], I16, side="right"))
        ig16 = me(nc.sbuf_tensor(f"{tag}ig16", [128, T], I16, side="right"))
        isv16 = me(nc.sbuf_tensor(f"{tag}isv16", [128, T], U16, side="right"))
        sigf = me(nc.sbuf_tensor(f"{tag}sigf", [128, T], F32, side="right"))
        xq_stack = ExitStack()
        xq = [xq_stack.enter_context(
            nc.sbuf_tensor(f"{tag}xq{k}", [128, T], BF16, side="right"))
            for k in range(KD)]

        tap_dbg = tapme(tag + "_dbg", [4, 16], F32)

        def dump_srow(row):
            if tap_dbg is not None:
                p.wait("y", "sv", p.val("sv"))
                st_dma(tap_dbg[row:row + 1, :], srow[0:1, :])

        p.phase(f"{tag}_ln")
        sv_reps = ln_stats(X2, xq, fin1[0], "plain", r12rep)
        p.wait("v", "sv", sv_reps)
        for k in range(KD):
            p.run("v", lambda v, k=k: v.tensor_tensor(
                lnf[:], X2[k][:], murep[:], Op.subtract))
            p.run("v", lambda v: v.tensor_tensor(lnf[:], lnf[:], r12rep[:], Op.mult))
            p.run("v", lambda v, k=k: v.tensor_copy(xq[k][:], lnf[:]))
            p.runw("v", lambda v: v.tensor_reduce(
                mcol[:, 2:3], lnf[:], AX.X, Op.max, apply_absolute_value=True))
            p.run("v", lambda v, k=k: v.tensor_tensor(
                mcol[:, 0:1], mcol[:, 2:3],
                mcol[:, 2:3] if k == 0 else mcol[:, 0:1], Op.max), "sv")
        p.wait("g", "sv", p.val("sv"))
        p.run("g", lambda g: g.tensor_reduce(srow[0:1, 0:1], mcol[:, 0:1],
                                             AX.XYZWC, Op.max), "sg")
        if stop == "pre_ar":
            p.phase(f"{tag}_stop")
            xq_stack.close()
            f_stack.close()
            return
        p.wait("y", "sg", p.val("sg"))
        dy_ar = all_reduce_max(1, srow)
        if stop == "post_ar":
            p.phase(f"{tag}_stop")
            p.wtok("y", dy_ar)
            xq_stack.close()
            f_stack.close()
            return
        p.wtok("v", dy_ar)
        p.runw("v", lambda v: v.tensor_scalar(
            srow[0:1, 1:2], srow[0:1, 0:1], 1.0 / 32767.0, 1e-12, Op.mult, Op.max))
        p.runw("v", lambda v: v.reciprocal(srow[0:1, 2:3], srow[0:1, 1:2]))
        dump_srow(0)
        repl_col(srow, 1, cols[:, 0:1])
        repl_col(srow, 2, cols[:, 1:2])
        sv_cols = p.val("sv")
        p.wait("v", "sv", sv_cols)
        for k in range(KD):
            p.run("v", lambda v, k=k: v.tensor_scalar(
                iu16[:], xq[k][:], cols[:, 1:2], None, Op.mult), "sv")
            p.wait("v", "sv", p.val("sv"))
            p.run("v", lambda v, k=k: v.tensor_scalar(
                xq[k][:], iu16[:], cols[:, 0:1], None, Op.mult), "sv")
        ss_xq = p.val("sv")
        if stop == "ln":
            p.phase(f"{tag}_stop")
            xq_stack.close()
            f_stack.close()
            return

        ug_dd = []
        for half in range(2):
            for wi, (wsrc, fdst) in enumerate([(wuqT, f_up), (wgqT, f_gate)]):
                if stop.startswith("ugq") and (half * 2 + wi) >= int(stop[3:]):
                    break
                p.phase(f"{tag}_ug{half}{wi}")
                toks_w = stream_weight_bf16(wsrc, half * 1536, 1536, wf)
                for tok in toks_w:
                    p.wtok("t", tok)
                p.wait("t", "sv", ss_xq)
                psv = {}
                for mm_ in range(12):
                    m = half * 12 + mm_
                    for tc in range(NT):
                        b = (mm_ * NT + tc) % 6
                        if b in psv:
                            p.wait("t", "sv", psv[b][0])
                            p.wait("t", "ss", psv[b][1])
                        for k in range(KD):
                            p.run("t", lambda t, mm_=mm_, tc=tc, k=k, b=b: t.matmul(
                                ps[b][:, :], wf[k][:, mm_ * 128:(mm_ + 1) * 128],
                                xq[k][:, tc * 512:(tc + 1) * 512],
                                start=(k == 0), stop=(k == KD - 1)),
                                "st" if k == KD - 1 else None)
                        st_mm = p.val("st")
                        rr = len(ug_dd)
                        i = rr % 2
                        if rr >= 2:
                            p.wtok("s", ug_dd[rr - 2][0])
                            p.wait("s", "sv", ug_dd[rr - 2][1])
                        p.wait("s", "st", st_mm)
                        ss_e = p.run("s", lambda s, i=i, b=b: s.activation(
                            fst[i][:], ps[b][:, :], AF.Copy), "ss")
                        p.wait("v", "ss", ss_e)
                        first = (half == 0 and mm_ == 0 and tc == 0)
                        p.runw("v", lambda v, i=i: v.tensor_reduce(
                            mcol[:, 2:3], fst[i][:], AX.X, Op.max,
                            apply_absolute_value=True))
                        sv_b = p.runw("v", lambda v, wi=wi, first=first:
                                      v.tensor_tensor(
                            mcol[:, wi:wi + 1], mcol[:, 2:3],
                            mcol[:, 2:3] if first else mcol[:, wi:wi + 1],
                            Op.max))
                        psv[b] = (sv_b, ss_e)
                        p.wait("y", "ss", ss_e)
                        ug_dd.append((st_dma(
                            fdst[m * 128:(m + 1) * 128, tc * 512:(tc + 1) * 512],
                            fst[i][:]), p.val("sv")))
        if stop == "ug" or stop.startswith("ugq"):
            p.phase(f"{tag}_stop")
            xq_stack.close()
            f_stack.close()
            return
        p.phase(f"{tag}_ugar")
        p.wait("g", "sv", p.val("sv"))
        p.run("g", lambda g: g.tensor_reduce(srow[0:1, 0:1], mcol[:, 0:1],
                                             AX.XYZWC, Op.max))
        p.run("g", lambda g: g.tensor_reduce(srow[0:1, 1:2], mcol[:, 1:2],
                                             AX.XYZWC, Op.max), "sg")
        p.wait("y", "sg", p.val("sg"))
        dy_ar = all_reduce_max(2, srow)
        p.wtok("v", dy_ar)
        p.runw("v", lambda v: v.tensor_scalar(
            srow[0:1, 2:3], srow[0:1, 0:1], 1.0 / 32767.0, 1e-12, Op.mult, Op.max))
        p.runw("v", lambda v: v.reciprocal(srow[0:1, 3:4], srow[0:1, 2:3]))
        p.runw("v", lambda v: v.tensor_scalar(
            srow[0:1, 4:5], srow[0:1, 1:2], 1.0 / 32767.0, 1e-12, Op.mult, Op.max))
        p.runw("v", lambda v: v.reciprocal(srow[0:1, 5:6], srow[0:1, 4:5]))
        dump_srow(1)
        repl_col(srow, 2, cols[:, 3:4])
        repl_col(srow, 3, cols[:, 4:5])
        repl_col(srow, 4, cols[:, 5:6])
        repl_col(srow, 5, cols[:, 6:7])
        sv_cols2 = p.val("sv")

        p.phase(f"{tag}_p2")
        xq_stack.close()
        p.dma_barrier("y", "st")
        p.wait("v", "sv", sv_cols2)
        done_v = {}
        done_d = {}
        prev_ss = 0
        prev_prod = 0
        prev_prod_sv = 0
        for k in range(KI):
            i = k % 2
            if k >= 2:
                p.wait("y", "sv", done_v[k - 2])
                for tok in done_d[k - 2]:
                    p.wtok("y", tok)
            du = ld(fin1[i][:], f_up[k * 128:(k + 1) * 128, :])
            if k >= 1:
                for tok in done_d[k - 1]:
                    p.wtok("y", tok)
                p.wait("y", "sv", prev_prod_sv)
            dg = ld(fin1[2][:], f_gate[k * 128:(k + 1) * 128, :])
            if k >= 1:
                for tok in done_d[k - 1]:
                    p.wtok("v", tok)
                p.wait("v", "ss", prev_ss)
            p.wtok("v", du)
            sv_u1 = p.run("v", lambda v, i=i: v.tensor_scalar(
                iu16[:], fin1[i][:], cols[:, 4:5], None, Op.mult), "sv")
            p.wtok("v", dg)
            sv_g1 = p.run("v", lambda v: v.tensor_scalar(
                ig16[:], fin1[2][:], cols[:, 6:7], None, Op.mult), "sv")
            p.wait("y", "sv", sv_g1)
            t_u1 = st_dma(f_u1[k * 128:(k + 1) * 128, :], iu16[:])
            t_g1 = st_dma(f_g1[k * 128:(k + 1) * 128, :], ig16[:])
            p.wait("s", "sv", sv_g1)
            if k >= 1:
                p.wait("s", "sv", prev_prod_sv)
            p.run("s", lambda s: s.activation(
                sigf[:], ig16[:], AF.Sigmoid, bias=0.0, scale=cols[:, 5:6]))
            prev_ss = p.run("s", lambda s: s.activation(
                lnf[:], ig16[:], AF.Copy), "ss")
            p.wait("v", "ss", prev_ss)
            sv_sval = p.run("v", lambda v: v.tensor_scalar(
                isv16[:], sigf[:], 65536.0, None, Op.mult), "sv")
            p.wait("y", "sv", sv_sval)
            t_sv = st_dma(f_sv[k * 128:(k + 1) * 128, :], isv16[:])
            p.run("v", lambda v: v.tensor_tensor(
                lnf[:], lnf[:], isv16[:], Op.mult))
            prev_prod = p.runw("v", lambda v: v.tensor_reduce(
                mcol[:, 2:3], lnf[:], AX.X, Op.max, apply_absolute_value=True))
            prev_prod_sv = prev_prod
            done_v[k] = p.run("v", lambda v, k=k: v.tensor_tensor(
                mcol[:, 0:1], mcol[:, 2:3],
                mcol[:, 2:3] if k == 0 else mcol[:, 0:1], Op.max), "sv")
            done_d[k] = (t_u1, t_g1, t_sv)
        if stop == "p2":
            mid_stack.close()
            f_stack.close()
            return
        p.phase(f"{tag}_p2ar")
        p.wait("g", "sv", p.val("sv"))
        p.run("g", lambda g: g.tensor_reduce(srow[0:1, 0:1], mcol[:, 0:1],
                                             AX.XYZWC, Op.max), "sg")
        p.wait("y", "sg", p.val("sg"))
        dy_ar = all_reduce_max(1, srow)
        p.wtok("v", dy_ar)
        p.runw("v", lambda v: v.tensor_scalar(
            srow[0:1, 1:2], srow[0:1, 0:1], 1.0 / 32767.0, 1e-30, Op.mult, Op.max))
        p.runw("v", lambda v: v.reciprocal(srow[0:1, 2:3], srow[0:1, 1:2]))
        repl_col(srow, 2, cols[:, 9:10])
        repl_col(srow, 1, cols[:, 13:14])
        p.wait("v", "sv", p.val("sv"))
        p.runw("v", lambda v: v.tensor_scalar(
            cols[:, 13:14], cols[:, 13:14], cols[:, 5:6], 1.0 / 65536.0,
            Op.mult, Op.mult))
        p.run("v", lambda v: v.tensor_scalar(
            cols[:, 13:14], cols[:, 13:14], cols[:, 3:4], None, Op.mult), "sv")
        sv_cols3 = p.val("sv")

        p.phase(f"{tag}_p3")
        p.dma_barrier("y", "st")
        p.wait("v", "sv", sv_cols3)
        done_v = {}
        done_d = {}
        prev_ssg = 0
        prev_ssg2 = 0
        prev_pp = 0
        for k in range(KI):
            i = k % 2
            u1t = iu16 if i == 0 else ig16
            if k >= 2:
                p.wait("y", "sv", done_v[k - 2])
                p.wtok("y", done_d[k - 2])
            dg1 = ld(fin1[i][:].bitcast(I16), f_g1[k * 128:(k + 1) * 128, :])
            if k >= 1:
                p.wait("y", "sv", prev_pp)
            dsv = ld(fin1[2][:].bitcast(U16), f_sv[k * 128:(k + 1) * 128, :])
            du1 = ld(u1t[:], f_u1[k * 128:(k + 1) * 128, :])
            p.wtok("s", dg1)
            if k >= 1:
                p.wait("s", "sv", prev_pp)
            prev_ssg = p.run("s", lambda s, i=i: s.activation(
                lnf[:], fin1[i][:].bitcast(I16), AF.Copy), "ss")
            p.wait("v", "ss", prev_ssg)
            p.wtok("v", dsv)
            p.run("v", lambda v: v.tensor_tensor(
                lnf[:], lnf[:], fin1[2][:].bitcast(U16), Op.mult))
            if k >= 1:
                p.wtok("v", done_d[k - 1])
                p.wait("v", "ss", prev_ssg2)
            sv_g2 = p.run("v", lambda v: v.tensor_scalar(
                isv16[:].bitcast(I16), lnf[:], cols[:, 9:10], None, Op.mult), "sv")
            p.wait("y", "sv", sv_g2)
            done_d[k] = st_dma(f_g2[k * 128:(k + 1) * 128, :],
                               isv16[:].bitcast(I16))
            p.wait("s", "sv", sv_g2)
            prev_ssg2 = p.run("s", lambda s: s.activation(
                sigf[:], isv16[:].bitcast(I16), AF.Copy), "ss")
            p.wait("v", "ss", prev_ssg2)
            p.wtok("v", du1)
            p.run("v", lambda v, u1t=u1t: v.tensor_tensor(
                lnf[:], sigf[:], u1t[:], Op.mult))
            prev_pp = p.runw("v", lambda v: v.tensor_reduce(
                mcol[:, 2:3], lnf[:], AX.X, Op.max, apply_absolute_value=True))
            done_v[k] = p.run("v", lambda v, k=k: v.tensor_tensor(
                mcol[:, 0:1], mcol[:, 2:3],
                mcol[:, 2:3] if k == 0 else mcol[:, 0:1], Op.max), "sv")
        if stop == "p3":
            mid_stack.close()
            f_stack.close()
            return
        p.phase(f"{tag}_p3ar")
        p.wait("g", "sv", p.val("sv"))
        p.run("g", lambda g: g.tensor_reduce(srow[0:1, 0:1], mcol[:, 0:1],
                                             AX.XYZWC, Op.max), "sg")
        p.wait("y", "sg", p.val("sg"))
        dy_ar = all_reduce_max(1, srow)
        p.wtok("v", dy_ar)
        p.runw("v", lambda v: v.tensor_scalar(
            srow[0:1, 1:2], srow[0:1, 0:1], 1.0 / 32767.0, 1e-30, Op.mult, Op.max))
        p.runw("v", lambda v: v.reciprocal(srow[0:1, 2:3], srow[0:1, 1:2]))
        repl_col(srow, 1, cols[:, 10:11])
        repl_col(srow, 2, cols[:, 11:12])
        sv_cols4 = p.val("sv")

        p.phase(f"{tag}_p4")
        p.dma_barrier("y", "st")
        p.wait("v", "sv", sv_cols4)
        done_s = {}
        done_d = {}
        prev_sg = 0
        for k in range(KI):
            i = k % 2
            i2 = 2 + (k % 2)
            u1t = iu16 if i == 0 else ig16
            if k >= 2:
                p.wait("y", "sv", done_s[k - 2])
                p.wtok("y", done_d[k - 2])
            dg2 = ld(fin1[i][:].bitcast(I16), f_g2[k * 128:(k + 1) * 128, :])
            du1 = ld(u1t[:], f_u1[k * 128:(k + 1) * 128, :])
            p.wtok("s", dg2)
            if k >= 1:
                p.wait("s", "sv", done_s[k - 1])
            prev_sg = p.run("s", lambda s, i=i: s.activation(
                lnf[:], fin1[i][:].bitcast(I16), AF.Copy), "ss")
            p.wait("v", "ss", prev_sg)
            p.wtok("v", du1)
            p.run("v", lambda v, u1t=u1t: v.tensor_tensor(
                lnf[:], lnf[:], u1t[:], Op.mult))
            p.run("v", lambda v: v.tensor_scalar(
                isv16[:].bitcast(I16), lnf[:], cols[:, 11:12], None, Op.mult))
            if k >= 1:
                p.wtok("v", done_d[k - 1])
            done_s[k] = p.run("v", lambda v: v.tensor_scalar(
                fin1[2][:], isv16[:].bitcast(I16), cols[:, 10:11],
                None, Op.mult), "sv")
            p.wait("y", "sv", done_s[k])
            done_d[k] = st_dma(f_o[k * 128:(k + 1) * 128, :], fin1[2][:])
        if stop == "p4":
            mid_stack.close()
            f_stack.close()
            return
        p.phase(f"{tag}_middone")
        p.dma_barrier("y", "st")
        p.dma_barrier("v", "st")
        mid_stack.close()
        wfd = [fe(nc.sbuf_tensor(f"{tag}wfd{k}", [128, 512], BF16, side="right"))
               for k in range(KI)]
        obuf = [fe(nc.sbuf_tensor(f"{tag}ob{i}", [128, 1024], BF16, side="right"))
                for i in range(2)]
        dn_sv = 0
        prev_st_dn = 0
        for mh in range(2):
            p.phase(f"{tag}_down{mh}")
            p.dma_barrier("y", "st")
            p.wait("s", "st", p.val("st"))
            p.wait("s", "sv", p.val("sv"))
            cast_wd2 = stream_weight(wdqT, mh * 512, 512, wfd)
            p.wait("t", "ss", cast_wd2)
            p.wait("t", "sv", p.val("sv"))
            for dummy, th in ([(True, 0), (False, 0), (False, 1)]
                              if DNDBG[0] else [(False, 0), (False, 1)]):
                p.wait("t", "sv", p.val("sv"))
                p.wait("t", "ss", p.val("ss"))
                ost = {}
                for k in range(KI):
                    i = k % 2
                    if k >= 2:
                        p.wait("y", "st", ost[k - 2])
                    elif prev_st_dn:
                        p.wait("y", "st", prev_st_dn)
                    do = ld(obuf[i][:], f_o[k * 128:(k + 1) * 128,
                                            th * 1024:(th + 1) * 1024])
                    p.wtok("t", do)
                    if k == 0 and dn_sv:
                        p.wait("t", "sv", dn_sv)
                    for mm_ in range(4):
                        for tcl in range(2):
                            b = mm_ * 2 + tcl
                            p.run("t", lambda t, i=i, mm_=mm_, tcl=tcl, b=b, k=k:
                                  t.matmul(
                                ps[b][:, :], wfd[k][:, mm_ * 128:(mm_ + 1) * 128],
                                obuf[i][:, tcl * 512:(tcl + 1) * 512],
                                start=(k == 0), stop=(k == KI - 1)),
                                "st" if (tcl == 1 and mm_ == 3) else None)
                    ost[k] = p.val("st")
                st_dn = p.val("st")
                prev_st_dn = st_dn
                if dummy:
                    continue
                p.wait("v", "st", st_dn)
                for mm_ in range(4):
                    m = mh * 4 + mm_
                    for tcl in range(2):
                        b = mm_ * 2 + tcl
                        tcg = th * 2 + tcl
                        p.run("v", lambda v, m=m, b=b, tcg=tcg:
                              v.scalar_tensor_tensor(
                            X2[m][:, tcg * 512:(tcg + 1) * 512], ps[b][:, :],
                            cols[:, 13:14],
                            X2[m][:, tcg * 512:(tcg + 1) * 512], Op.mult, Op.add),
                            "sv")
                dn_sv = p.val("sv")
        tap_cols = tapme(tag + "_cols", [128, 16], F32)
        p.phase(f"{tag}_taps")
        if tap_cols is not None:
            st_dma(tap_cols[:, :], cols[:, :])
        for nm, src_d, shp, dt in [("wdq", wdqT, [I3, D], F32),
                                   ("u1", f_u1, [I3, T], I16),
                                   ("g1", f_g1, [I3, T], I16),
                                   ("svq", f_sv, [I3, T], U16),
                                   ("g2", f_g2, [I3, T], I16),
                                   ("fo", f_o, [I3, T], BF16)]:
            tp = tapme(tag + "_" + nm, shp, dt)
            if tp is not None:
                p.dma_barrier("y", "st")
                st_dma(tp[:, :], src_d[:, :])
        p.dma_barrier("y", "st")
        p.phase(f"{tag}_end")
        f_stack.close()


    def emit_attn_b():
        tapme("ab_lnf", [128, T], F32)
        tapme("ab_mx", [128, 16], F32)
        tapme("ab_P", [128, T])
        tapme("ab_q", [64, 128])
        tapme("ab_k", [64, T])
        a_stack = ExitStack()
        ae = a_stack.enter_context
        xb = [ae(nc.sbuf_tensor(f"ab_x{k}", [128, T], BF16, side="right"))
              for k in range(KD)]
        w16a = [ae(nc.sbuf_tensor(f"ab_w{k}", [128, 1024], BF16, side="right"))
                for k in range(KD)]
        w16o = [ae(nc.sbuf_tensor(f"ab_wo{k}", [128, 1024], BF16, side="right"))
                for k in range(KD)]
        kgt = [ae(nc.sbuf_tensor(f"ab_k{g}", [64, T], BF16, side="right"))
               for g in range(NKV)]
        vtm = ae(nc.sbuf_tensor("ab_v", [128, (T // 128) * 256], BF16,
                                side="right"))
        Psb = ae(nc.sbuf_tensor("ab_p", [128, T], BF16, side="right"))
        P0 = ae(nc.sbuf_tensor("ab_p0", [128, 512], F32, side="right"))
        ngm = ae(nc.sbuf_tensor("ab_ngm", [128, 128], F32, side="right"))
        qhb = [ae(nc.sbuf_tensor(f"ab_q{i}", [64, 128], BF16, side="right"))
               for i in range(2)]
        PTs = [ae(nc.sbuf_tensor(f"ab_pt{i}", [128, 128], BF16, side="right"))
               for i in range(2)]
        oa8 = [ae(nc.sbuf_tensor(f"ab_oa{j}", [128, 128], BF16, side="right"))
               for j in range(KD)]
        mxc = ae(nc.sbuf_tensor("ab_mx", [128, 16], F32, side="right"))
        mx2 = ae(nc.sbuf_tensor("ab_mx2", [128, 16], F32, side="right"))
        zri = ae(nc.sbuf_tensor("ab_z", [1, 256], F32, side="right"))
        zrp = ae(nc.sbuf_tensor("ab_zr", [128, 128], F32, side="right"))

        p.phase("ab_ln")
        sv_reps = ln_stats(X2, xb, Psb, "plain", r12rep)
        p.wait("v", "sv", sv_reps)
        for k in range(KD):
            p.run("v", lambda v, k=k: v.tensor_tensor(
                lnf[:], X2[k][:], murep[:], Op.subtract))
            p.run("v", lambda v, k=k: v.tensor_tensor(
                xb[k][:], lnf[:], r12rep[:], Op.mult), "sv")
        sv_xb = p.val("sv")
        p.run("g", lambda g: g.iota(P0[:], [[1, 512]], base=0,
                                    channel_multiplier=-1,
                                    allow_small_or_imprecise_dtypes=True))
        p.run("g", lambda g: g.iota(ngm[:], [[1, 128]], base=0,
                                    channel_multiplier=-1,
                                    allow_small_or_imprecise_dtypes=True), "sg")
        p.wait("v", "sg", p.val("sg"))
        p.run("v", lambda v: v.tensor_scalar(
            ngm[:], ngm[:], 0.0, -30000.0, Op.is_gt, Op.mult), "sv")
        sv_cst = p.val("sv")

        p.phase("ab_kv")
        cast_kv = stream_weight(WkvT, 0, 512, w16a)
        p.wait("t", "ss", cast_kv)
        p.wait("t", "sv", sv_xb)
        prev_ev = {}
        for g in range(NKV):
            for tc in range(NT):
                b = (g * NT + tc) % 2
                if b in prev_ev:
                    p.wait("t", "sv", prev_ev[b])
                for k in range(KD):
                    p.run("t", lambda t, g=g, tc=tc, k=k, b=b: t.matmul(
                        ps[b][0:64, :], w16a[k][:, g * 64:(g + 1) * 64],
                        xb[k][:, tc * 512:(tc + 1) * 512],
                        start=(k == 0), stop=(k == KD - 1)),
                        "st" if k == KD - 1 else None)
                st_k = p.val("st")
                p.wait("v", "st", st_k)
                prev_ev[b] = p.run("v", lambda v, g=g, tc=tc, b=b: v.tensor_copy(
                    kgt[g][:, tc * 512:(tc + 1) * 512], ps[b][0:64, :]), "sv")
        for mc in range(T // 128):
            b = 2 + mc % 2
            if b in prev_ev:
                p.wait("t", "sv", prev_ev[b])
            for k in range(KD):
                p.run("t", lambda t, mc=mc, k=k, b=b: t.matmul(
                    ps[b][:, 0:256], xb[k][:, mc * 128:(mc + 1) * 128],
                    w16a[k][:, 256:512],
                    start=(k == 0), stop=(k == KD - 1)),
                    "st" if k == KD - 1 else None)
            st_v = p.val("st")
            p.wait("v", "st", st_v)
            prev_ev[b] = p.run("v", lambda v, mc=mc, b=b: v.tensor_copy(
                vtm[:, mc * 256:(mc + 1) * 256], ps[b][:, 0:256]), "sv")
        sv_kv = p.val("sv")

        p.phase("ab_wq")
        p.wait("s", "st", p.val("st"))
        cast_q = stream_weight(WqT, 0, 1024, w16a, scale=QSCALE)
        cast_o = stream_weight(WoT, 0, 1024, w16o)

        p.phase("ab_attn")
        p.wait("t", "ss", cast_o)
        p.wait("t", "sv", sv_cst)
        slopes = [2.0 ** (-8.0 / NH * (h + 1)) for h in range(NH)]
        qk_prev_st = {}
        sbank_sv = {}
        ptslot_ss = {}
        pts_st = {}
        oa_sv = {}
        zrep_st = {}
        zrp_sv = {}
        ss_exp = 0
        st_lastPT = 0
        wo_st = 0
        for ci in range(T // 128):
            m_end = 128 * (ci + 1)
            nmb = (m_end + 511) // 512
            for h in range(NH):
                g = h // 4
                qi = h % 2
                par = h % 2
                slope = slopes[h]
                j8 = h // 2
                if qi in qk_prev_st:
                    p.wait("v", "st", qk_prev_st[qi])
                if (5 in prev_ev):
                    p.wait("t", "sv", prev_ev[5])
                for k in range(KD):
                    p.run("t", lambda t, h=h, ci=ci, k=k: t.matmul(
                        ps[5][0:64, 0:128], w16a[k][:, h * 64:(h + 1) * 64],
                        xb[k][:, ci * 128:(ci + 1) * 128],
                        start=(k == 0), stop=(k == KD - 1)),
                        "st" if k == KD - 1 else None)
                st_q = p.val("st")
                p.wait("v", "st", st_q)
                prev_ev[5] = p.run("v", lambda v, qi=qi: v.tensor_copy(
                    qhb[qi][:], ps[5][0:64, 0:128]), "sv")
                sv_q = prev_ev[5]
                p.wait("t", "sv", sv_q)
                if ss_exp:
                    p.wait("v", "ss", ss_exp)
                for mb in range(nmb):
                    w = min(512, m_end - mb * 512)
                    b = mb % 2
                    if b in sbank_sv:
                        p.wait("t", "sv", sbank_sv[b])
                    p.run("t", lambda t, qi=qi, g=g, mb=mb, w=w, b=b: t.matmul(
                        ps[b][:, 0:w], qhb[qi][:],
                        kgt[g][:, mb * 512:mb * 512 + w],
                        start=True, stop=True), "st")
                    st_s = p.val("st")
                    p.wait("v", "st", st_s)
                    p.run("v", lambda v, mb=mb, w=w, b=b, slope=slope:
                          v.scalar_tensor_tensor(
                        lnf[:, mb * 512:mb * 512 + w], P0[:, 0:w], slope,
                        ps[b][:, 0:w], Op.mult, Op.add))
                    if mb == nmb - 1:
                        p.run("v", lambda v, m_end=m_end: v.tensor_tensor(
                            lnf[:, m_end - 128:m_end], lnf[:, m_end - 128:m_end],
                            ngm[:], Op.add))
                    nb32 = w // 32
                    p.runw("v", lambda v, mb=mb, w=w, nb32=nb32: v.tensor_reduce(
                        mx2[:, 0:nb32].unsqueeze(2),
                        lnf[:, mb * 512:mb * 512 + w].rearrange(
                            "p (b k) -> p b k", k=32),
                        AX.X, Op.max))
                    p.runw("v", lambda v, mb=mb, nb32=nb32: v.tensor_reduce(
                        mxc[:, mb:mb + 1], mx2[:, 0:nb32], AX.X, Op.max))
                    dlt = slope * (512.0 * mb - 128.0 * ci)
                    sbank_sv[b] = p.run("v", lambda v, mb=mb, dlt=dlt:
                                        v.tensor_scalar(
                        mxc[:, mb:mb + 1], mxc[:, mb:mb + 1], 1.0, dlt,
                        Op.mult, Op.add), "sv")
                qk_prev_st[qi] = p.val("st")
                p.wait("v", "sv", p.val("sv"))
                p.runw("v", lambda v, nmb=nmb: v.tensor_reduce(
                    mxc[:, 6:7], mxc[:, 0:nmb], AX.X, Op.max))
                for mb in range(nmb):
                    dlt = slope * (512.0 * mb - 128.0 * ci)
                    p.run("v", lambda v, mb=mb, dlt=dlt: v.tensor_scalar(
                        mxc[:, 8 + mb:9 + mb], mxc[:, 6:7], -1.0, dlt,
                        Op.mult, Op.add), "sv")
                sv_bc = p.val("sv")
                p.wait("s", "sv", sv_bc)
                if st_lastPT:
                    p.wait("s", "st", st_lastPT)
                for mb in range(nmb):
                    w = min(512, m_end - mb * 512)
                    p.run("s", lambda s, mb=mb, w=w: s.activation(
                        Psb[:, mb * 512:mb * 512 + w],
                        lnf[:, mb * 512:mb * 512 + w], AF.Exp,
                        bias=mxc[:, 8 + mb:9 + mb], scale=1.0), "ss")
                ss_exp = p.val("ss")
                if ci == ABTAP[0] and h == ABTAP[1] and "ab_lnf" in TAPS:
                    p.wait("y", "ss", ss_exp)
                    tks = [st_dma(TAPS["ab_lnf"][:], lnf[:]),
                           st_dma(TAPS["ab_mx"][:], mxc[:]),
                           st_dma(TAPS["ab_P"][:], Psb[:]),
                           st_dma(TAPS["ab_q"][:], qhb[qi][:]),
                           st_dma(TAPS["ab_k"][:], kgt[g][:])]
                    for e_ in ("s", "v", "t"):
                        for tk_ in tks:
                            p.wtok(e_, tk_)
                if par in oa_sv:
                    p.wait("t", "sv", oa_sv[par])
                p.wait("t", "ss", ss_exp)
                for mc in range(ci + 1):
                    slot = mc % 4
                    pb = mc % 2
                    if slot in ptslot_ss:
                        p.wait("t", "ss", ptslot_ss[slot])
                    p.run("t", lambda t, mc=mc, slot=slot: t.transpose(
                        ps[2][:, slot * 64:(slot + 1) * 64].bitcast(BF16),
                        Psb[:, mc * 128:(mc + 1) * 128], identb[:]), "st")
                    st_tr = p.val("st")
                    if pb in pts_st:
                        p.wait("s", "st", pts_st[pb])
                    p.wait("s", "st", st_tr)
                    ptslot_ss[slot] = p.run(
                        "s", lambda s, slot=slot, pb=pb: s.activation(
                            PTs[pb][:],
                            ps[2][:, slot * 64:(slot + 1) * 64].bitcast(BF16),
                            AF.Copy), "ss")
                    p.wait("t", "ss", ptslot_ss[slot])
                    p.run("t", lambda t, mc=mc, g=g, pb=pb, par=par, ci=ci:
                          t.matmul(
                        ps[3][par * 64:(par + 1) * 64, 0:128],
                        vtm[:, mc * 256 + g * 64:mc * 256 + (g + 1) * 64],
                        PTs[pb][:], start=(mc == 0), stop=(mc == ci)))
                    p.run("t", lambda t, mc=mc, pb=pb, par=par, ci=ci: t.matmul(
                        ps[4][0:1, par * 128:par * 128 + 128],
                        onescol[:, 0:1], PTs[pb][:],
                        start=(mc == 0), stop=(mc == ci)), "st")
                    pts_st[pb] = p.val("st")
                st_lastPT = p.val("st")
                if par in zrep_st:
                    p.wait("v", "st", zrep_st[par])
                p.wait("v", "st", st_lastPT)
                p.run("v", lambda v, par=par: v.tensor_copy(
                    zri[0:1, par * 128:par * 128 + 128],
                    ps[4][0:1, par * 128:par * 128 + 128]), "sv")
                p.wait("v", "sv", p.val("sv"))
                sv_zr = p.runw("v", lambda v, par=par: v.reciprocal(
                    zri[0:1, par * 128:par * 128 + 128],
                    zri[0:1, par * 128:par * 128 + 128]))
                p.wait("t", "sv", sv_zr)
                p.run("t", lambda t, par=par: t.matmul(
                    ps[4][par * 64:(par + 1) * 64, 384:512],
                    onesf[0:1, 0:64],
                    zri[0:1, par * 128:par * 128 + 128],
                    start=True, stop=True), "st")
                zrep_st[par] = p.val("st")
                p.wait("s", "st", zrep_st[par])
                if par in zrp_sv:
                    p.wait("s", "sv", zrp_sv[par])
                ss_zrp = p.run("s", lambda s, par=par: s.activation(
                    zrp[par * 64:(par + 1) * 64, :],
                    ps[4][par * 64:(par + 1) * 64, 384:512], AF.Copy), "ss")
                p.wait("v", "ss", ss_zrp)
                if wo_st:
                    p.wait("v", "st", wo_st)
                oa_sv[par] = p.run("v", lambda v, j8=j8, par=par: v.tensor_tensor(
                    oa8[j8][par * 64:(par + 1) * 64, :],
                    ps[3][par * 64:(par + 1) * 64, 0:128],
                    zrp[par * 64:(par + 1) * 64, :], Op.mult), "sv")
                zrp_sv[par] = oa_sv[par]
            p.wait("t", "sv", p.val("sv"))
            wo_sv_prev = 0
            for m in range(KD):
                b = 6 + m % 2
                if wo_sv_prev and m >= 2:
                    p.wait("t", "sv", wo_sv_prev)
                for j in range(KD):
                    p.run("t", lambda t, m=m, j=j, b=b: t.matmul(
                        ps[b][:, 0:128], w16o[j][:, m * 128:(m + 1) * 128],
                        oa8[j][:], start=(j == 0), stop=(j == KD - 1)),
                        "st" if j == KD - 1 else None)
                st_wo = p.val("st")
                p.wait("v", "st", st_wo)
                wo_sv_prev = p.run("v", lambda v, m=m, ci=ci, b=b: v.tensor_tensor(
                    X2[m][:, ci * 128:(ci + 1) * 128],
                    X2[m][:, ci * 128:(ci + 1) * 128],
                    ps[b][:, 0:128], Op.add), "sv")
            wo_st = p.val("st")
        p.phase("ab_end")
        a_stack.close()

    if UPTO[0] != "attn_a":
        emit_ffn("f1")
    if UPTO[0] in ("all", "attnb"):
        emit_attn_b()
    if UPTO[0] == "all":
        emit_ffn("f2")

    p.phase("pout")
    p.wait("y", "sv", p.val("sv"))
    for k in range(KD):
        st_dma(out_fm[k * 128:(k + 1) * 128, :], X2[k][:])
    p.dma_barrier("y", "st")
    p.flush()
    gstack.close()
    return nc


def _perm_kv(Wkv):
    rows = [Wkv[g * 128:g * 128 + 64] for g in range(NKV)]
    rows += [Wkv[g * 128 + 64:(g + 1) * 128] for g in range(NKV)]
    return np.ascontiguousarray(np.concatenate(rows, axis=0).T)


def prep_inputs(inputs):
    x = inputs["L0_attn_a_x"]
    base = {
        "w_norm": np.ascontiguousarray(inputs["L0_attn_a_w_norm"].reshape(KD, 128)),
        "b_norm": np.ascontiguousarray(inputs["L0_attn_a_b_norm"].reshape(KD, 128)),
        "w_in": np.ascontiguousarray(inputs["L0_attn_a_w_in"]),
        "w_conv": np.ascontiguousarray(inputs["L0_attn_a_w_conv"].reshape(E, 4)),
        "b_conv": np.ascontiguousarray(inputs["L0_attn_a_b_conv"].reshape(KE, 128)),
        "A_log": np.ascontiguousarray(inputs["L0_attn_a_A_log"]),
        "w_b": np.ascontiguousarray(inputs["L0_attn_a_w_b"]),
        "w_c": np.ascontiguousarray(inputs["L0_attn_a_w_c"]),
        "w_delta": np.ascontiguousarray(inputs["L0_attn_a_w_delta"]),
        "b_delta": np.ascontiguousarray(inputs["L0_attn_a_b_delta"].reshape(KE, 128)),
        "D_param": np.ascontiguousarray(inputs["L0_attn_a_D_param"].reshape(KE, 128)),
        "w_out": np.ascontiguousarray(inputs["L0_attn_a_w_out"]),
        "WqT": np.ascontiguousarray(inputs["L0_attn_b_Wq"].T),
        "WkvT": _perm_kv(inputs["L0_attn_b_Wkv"]),
        "WoT": np.ascontiguousarray(inputs["L0_attn_b_Wo"].T),
        "wg_n": np.ascontiguousarray(inputs["L0_ffn_w_gate"]),
        "wu_n": np.ascontiguousarray(inputs["L0_ffn_w_up"]),
        "wd_n": np.ascontiguousarray(inputs["L0_ffn_w_down"]),
    }
    maps = []
    for c in range(NCORES):
        m = dict(base)
        m["x_fm"] = np.ascontiguousarray(x[c].T)
        maps.append(m)
    return maps


def kernel(**inputs):
    nc = build_program()
    maps = prep_inputs(inputs)
    res = run_bass_kernel_spmd(nc, maps, core_ids=list(range(NCORES)))
    return np.stack([np.ascontiguousarray(np.asarray(r["out_fm"]).T)
                     for r in res.results])
